# version 04 (final)
# baseline (speedup 1.0000x reference)
import contextlib
import numpy as np
import ml_dtypes
import concourse.bass as bass
import concourse.mybir as mybir
from concourse.bass_utils import run_bass_kernel_spmd

F32 = mybir.dt.float32
BF16 = mybir.dt.bfloat16
AF = mybir.ActivationFunctionType
ALU = mybir.AluOpType
AX = mybir.AxisListType.X

D = 2048
H = 16
DFF = 8192
NCORE = 8
EPS32 = float(np.finfo(np.float32).eps)
DSCALE = float(128.0 ** -0.5)
QB = 128.0
EPSF = 1e-6
bfd = ml_dtypes.bfloat16


class Prog:
    DSTREAMS = ("lc", "lm0", "lm1", "lx", "lb", "lw0", "lw1", "ls0", "ls1",
                "lg0", "lg1", "lh60", "lh61", "lf0", "lf1", "lr",
                "w1", "we0", "we1", "wc", "wo0", "wo1", "w5", "wp5",
                "wf0", "wf1", "wr", "w2", "lss")

    def __init__(self):
        self.enabled = True
        self.ops = {k: [] for k in ("sync", "gpsimd", "vector", "scalar", "tensor")}
        self.sem = {}
        self.cnt = {"pe": 0, "ve": 0, "ac": 0, "po": 0, "cc": 0}
        for s in self.DSTREAMS:
            self.cnt[s] = 0
        self.key = {"vector": "ve", "scalar": "ac", "tensor": "pe", "gpsimd": "po"}

    def wait(self, eng, sname, val):
        if not self.enabled:
            return
        if val and val > 0:
            self.ops[eng].append(lambda e, s=sname, v=val: e.wait_ge(self.sem[s], v))

    def wtok(self, eng, tok):
        if tok:
            self.wait(eng, tok[0], tok[1])

    def dma_in(self, stream, out_ap, in_ap, transpose=False):
        if not self.enabled:
            return (stream, 0)
        self.cnt[stream] += 16
        self.ops["sync"].append(
            lambda e, o=out_ap, i=in_ap, tr=transpose, s=stream:
            e.dma_start(out=o, in_=i, transpose=tr).then_inc(self.sem[s], 16))
        return (stream, self.cnt[stream])

    def dma_out(self, stream, out_ap, in_ap):
        if not self.enabled:
            return (stream, 0)
        self.cnt[stream] += 16
        self.ops["gpsimd"].append(
            lambda e, o=out_ap, i=in_ap, s=stream:
            e.dma_start(out=o, in_=i).then_inc(self.sem[s], 16))
        return (stream, self.cnt[stream])

    def op(self, eng, fn):
        if not self.enabled:
            return 0
        k = self.key[eng]
        prev = self.cnt[k]
        if prev > 0:
            self.ops[eng].append(
                lambda e, s=k, v=prev: e.wait_ge(self.sem[s], v))
        self.cnt[k] += 1
        self.ops[eng].append(lambda e, f=fn, s=k: f(e).then_inc(self.sem[s], 1))
        return self.cnt[k]

    def wait_all_dma(self, eng, fam=None):
        for s in self.DSTREAMS:
            if fam and not s.startswith(fam):
                continue
            self.wait(eng, s, self.cnt[s])

    def barrier(self):
        if not self.enabled:
            return
        snap = dict(self.cnt)
        own = {"sync": None, "gpsimd": "po", "vector": "ve", "scalar": "ac",
               "tensor": "pe"}
        for eng in self.ops:
            for s in ("pe", "ve", "ac", "po"):
                if s == own[eng]:
                    continue
                self.wait(eng, s, snap[s])
            for s in self.DSTREAMS:
                self.wait(eng, s, snap[s])


DBG = False


def build(nt, part, n_res_cap=2048):
    nch = nt // 128
    n_res = min(nt, n_res_cap)
    n_half = nt // n_res
    n_kilo = n_res // 1024
    ng = nt // 512
    ntc_res = n_res // 128

    nc = bass.Bass("TRN2", target_bir_lowering=False, debug=False)
    P = Prog()
    W, OP, DIN, DOUT = P.wait, P.op, P.dma_in, P.dma_out

    dtt = nc.dram_tensor
    x_tok = dtt("x_tok", [nt, D], F32, kind="ExternalInput").ap()
    x_fm = dtt("x_fm", [H, 128, nt], F32, kind="ExternalInput").ap()
    wq_t = dtt("wq_t", [16, 16, 128, 128], F32, kind="ExternalInput").ap()
    wkv_t = dtt("wkv_t", [8, 16, 128, 512], F32, kind="ExternalInput").ap()
    wp_t = dtt("wp_t", [16, 16, 128, 128], F32, kind="ExternalInput").ap()
    wfc_t = dtt("wfc_t", [64, 16, 128, 128], F32, kind="ExternalInput").ap()
    wpf_t = dtt("wpf_t", [16, 64, 128, 128], F32, kind="ExternalInput").ap()
    qn_w = dtt("qn_w", [1, D], F32, kind="ExternalInput").ap()
    n_w = dtt("n_w", [128, 16], F32, kind="ExternalInput").ap()
    pn_w = dtt("pn_w", [128, 16], F32, kind="ExternalInput").ap()
    s_vec = dtt("s_vec", [1, H], F32, kind="ExternalInput").ap()
    diffc = dtt("diffc", [128, 128], F32, kind="ExternalInput").ap()
    causc = dtt("causc", [128, 128], F32, kind="ExternalInput").ap()
    rowtc = dtt("rowtc", [128, 128], F32, kind="ExternalInput").ap()
    colc = dtt("colc", [128, 2], F32, kind="ExternalInput").ap()
    econ = dtt("econ", [1, 8], F32, kind="ExternalInput").ap()
    emask = dtt("emask", [1, 8], F32, kind="ExternalInput").ap()
    invn = dtt("invn", [1, 4], F32, kind="ExternalInput").ap()
    cvec = dtt("cvec", [1, 8], F32, kind="ExternalInput").ap()
    onecf = dtt("onecf", [128, 1], F32, kind="ExternalInput").ap()
    onecb = dtt("onecb", [128, 1], BF16, kind="ExternalInput").ap()
    onerf = dtt("onerf", [1, 128], F32, kind="ExternalInput").ap()

    out_fm = dtt("out_fm", [H, 128, nt], F32, kind="ExternalOutput").ap()

    ab_kind = "ExternalOutput" if part == "A" else "ExternalInput"
    h_attn_s = dtt("h_attn_s", [nt, D], BF16).ap()
    xh1_s = dtt("xh1_s", [nt, D], BF16, kind=ab_kind).ap()
    q_s = dtt("q_s", [H, 128, nt], BF16, kind=ab_kind).ap()
    k_s = dtt("k_s", [nt, D], BF16, kind=ab_kind).ap()
    v_s = dtt("v_s", [nt, D], BF16, kind=ab_kind).ap()
    sums_io = dtt("sums_io", [1, 4], F32, kind=ab_kind).ap()
    attno_s = dtt("attno_s", [H, 128, nt], BF16, kind="ExternalOutput" if (part == "B" and DBG) else "Internal").ap()
    s_s = dtt("s_s", [H, 128, nt], BF16, kind="ExternalOutput" if (part == "B" and DBG) else "Internal").ap()
    h_s = dtt("h_s", [64, 128, nt], BF16, kind="ExternalOutput" if (part == "B" and DBG) else "Internal").ap()
    wpf_bt = dtt("wpf_bt", [16, 64, 128, 128], BF16)
    wpf_b = wpf_bt.ap()
    stat_d = dtt("stat_d", [128, 32], F32).ap()
    cc_in = dtt("cc_in", [H * 128, 128], F32, kind="ExternalOutput" if part == "A" else "Internal").ap()
    cc_out = dtt("cc_out", [NCORE * H * 128, 128], F32, kind="ExternalInput" if part == "B" else "Internal").ap()
    red_in = dtt("red_in", [1, 4], F32).ap()
    red_out = dtt("red_out", [1, 4], F32).ap()

    k4d = k_s.rearrange("(c p) (h d) -> c h p d", p=128, d=128)
    v4d = v_s.rearrange("(c p) (h d) -> c h p d", p=128, d=128)

    stack = contextlib.ExitStack()
    sbt = lambda n, s, d: stack.enter_context(nc.sbuf_tensor(n, s, d))
    pst = lambda n, s, d: stack.enter_context(nc.psum_tensor(n, s, d))

    with stack:
        big = sbt("big", [128, max(16 * n_res, 32768)], BF16)
        masks = sbt("masks", [128, H * 128], F32)
        lamc = sbt("lamc", [128, H * 128], F32)
        Sf = sbt("Sf", [128, H * 128], F32)
        Sb = sbt("Sb", [128, H * 128], BF16)
        scr32 = sbt("scr32", [128, 2048], F32)
        wf = sbt("wf", [128, 2048], F32)
        wsg = sbt("wsg", [128, 2048], BF16)
        wtb = sbt("wtb", [128, 8192], BF16)
        evb = sbt("evb", [128, 2048], BF16)
        sqb = sbt("sqb", [128, 2048], BF16)
        rows = sbt("rows", [1, 2048], F32)
        rowt2 = sbt("rowt2", [1, 2048], F32)
        macc = sbt("macc", [128, 48], F32)
        bct = sbt("bct", [128, 2048], F32)
        qg2 = sbt("qg2", [128, 1024], BF16)
        kf2 = sbt("kf2", [128, 1024], BF16)
        kt2 = sbt("kt2", [128, 1024], BF16)
        vt2 = sbt("vt2", [128, 1024], BF16)
        qtb = sbt("qtb", [128, 128], BF16)
        stb = sbt("stb", [128, 128], BF16)
        ktb = sbt("ktb", [128, 128], BF16)
        och = sbt("och", [128, 1024], BF16)
        ev2 = sbt("ev2", [128, 2048], BF16)
        lamr = sbt("lamr", [128, H], F32)
        lamT = sbt("lamT", [128, H], F32)
        segc = sbt("segc", [128, H * 8], F32)
        gam = sbt("gam", [128, 8], F32)
        ngm = sbt("ngm", [128, 8], F32)
        srn = sbt("srn", [128, H], F32)
        nwc = sbt("nwc", [128, 16], F32)
        pwc = sbt("pwc", [128, 16], F32)
        w12 = sbt("w12", [128, 16], F32)
        sc1 = sbt("sc1", [128, 16], F32)
        smf = sbt("smf", [1, 96], F32)
        cvb = sbt("cvb", [128, 8], F32)
        reds = sbt("reds", [128, 8], F32)
        onc = sbt("onc", [128, 1], F32)
        onb = sbt("onb", [128, 1], BF16)
        onr = sbt("onr", [1, 128], F32)
        ecb = sbt("ecb", [128, 16], F32)
        cnst = sbt("cnst", [128, 386], F32)
        sgt = sbt("sgt", [128, 256], F32)
        ssb = sbt("ssb", [128, 1024], BF16)
        xfb = sbt("xfb", [128, 1024], F32)
        fob = sbt("fob", [128, 1024], F32)

        pa = pst("pa", [128, 1024], F32)
        pb = pst("pb", [128, 1024], F32)
        pc = pst("pc", [128, 512], F32)
        pt1 = pst("pt1", [1, 512], F32)
        pt2 = pst("pt2", [1, 512], F32)

        for s in ("pe", "ve", "ac", "po", "cc") + Prog.DSTREAMS:
            P.sem[s] = stack.enter_context(nc.semaphore(s))

        DIN("lc", cnst[:, 0:128], diffc)
        DIN("lc", cnst[:, 128:256], causc)
        DIN("lc", cnst[:, 256:384], rowtc)
        DIN("lc", cnst[:, 384:386], colc)
        DIN("lc", onc[:, :], onecf)
        DIN("lc", onb[:, :], onecb)
        DIN("lc", onr[:, :], onerf)
        DIN("lc", smf[:1, 0:16], s_vec)
        DIN("lc", smf[:1, 16:24], econ)
        DIN("lc", smf[:1, 24:32], emask)
        DIN("lc", smf[:1, 32:36], invn)
        DIN("lc", nwc[:, :], n_w)
        DIN("lc", smf[:1, 64:72], cvec)
        l_cn = DIN("lc", pwc[:, :], pn_w)

        if part == "A":
            v_last = OP("vector", lambda e: e.memset(reds[:, 0:4], 0.0))
            ms_par_v = [0, 0]
            flat_srcs = []
            wqf = wq_t.rearrange("a b p q -> (a b) p q")
            wkf = wkv_t.rearrange("a b p q -> (a b) p q")
            wpf2 = wp_t.rearrange("a b p q -> (a b) p q")
            wff = wfc_t.rearrange("a b p q -> (a b) p q")
            wpff = wpf_t.rearrange("a b p q -> (a b) p q")
            for j in range(32):
                flat_srcs.append((0, wqf[j * 8:(j + 1) * 8], 8, 128))
            for j in range(64):
                flat_srcs.append((0, wkf[j * 2:(j + 1) * 2], 2, 512))
            for j in range(32):
                flat_srcs.append((1, wpf2[j * 8:(j + 1) * 8], 8, 128))
            for j in range(128):
                flat_srcs.append((2, wff[j * 8:(j + 1) * 8], 8, 128))
            for j in range(128):
                flat_srcs.append((3, wpff[j * 8:(j + 1) * 8], 8, 128))
            for j, (wi, srcap, nb, wcol) in enumerate(flat_srcs):
                par = j % 2
                W("sync", "ve", ms_par_v[par])
                lw = DIN("lm0" if par == 0 else "lm1",
                         wf[:, par * 1024:(par + 1) * 1024]
                         .rearrange("p (b q) -> p b q", b=nb),
                         srcap.rearrange("b p q -> p b q"))
                P.wtok("vector", lw)
                OP("vector", lambda e, p=par, w=wi: e.tensor_reduce(
                    reds[:, 4 + w:5 + w], wf[:, p * 1024:(p + 1) * 1024], AX,
                    ALU.add, apply_absolute_value=True))
                v_last = OP("vector", lambda e, w=wi: e.tensor_tensor(
                    reds[:, w:w + 1], reds[:, w:w + 1], reds[:, 4 + w:5 + w],
                    ALU.add))
                ms_par_v[par] = v_last
            W("tensor", "ve", v_last)
            P.wtok("tensor", l_cn)
            pe_red = OP("tensor", lambda e: e.matmul(pc[:1, 0:4], onc[:, :],
                                                     reds[:, 0:4], start=True,
                                                     stop=True))
            W("vector", "pe", pe_red)
            v_red = OP("vector", lambda e: e.tensor_copy(smf[:1, 44:48],
                                                         pc[:1, 0:4]))
            W("gpsimd", "ve", v_red)
            DOUT("wc", sums_io[:, :], smf[:1, 44:48])
            l_ro = None
        else:
            l_ro = DIN("lss", smf[:1, 44:48], sums_io[:, :])

        if l_ro is not None:
            P.wtok("vector", l_ro)
            P.wtok("vector", l_cn)
        OP("vector", lambda e: e.tensor_tensor(smf[:1, 48:52], smf[:1, 44:48],
                                               smf[:1, 32:36], ALU.mult))
        OP("vector", lambda e: e.tensor_scalar(smf[:1, 48:50], smf[:1, 48:50],
                                               1e-5, None, ALU.max))
        OP("vector", lambda e: e.tensor_scalar(smf[:1, 52:54], smf[:1, 48:50],
                                               0.5, None, ALU.mult))
        OP("vector", lambda e: e.tensor_scalar(smf[:1, 54:56], smf[:1, 50:52],
                                               1e-6, 0.5, ALU.add, ALU.mult))
        v_sc = OP("vector", lambda e: e.tensor_scalar(smf[:1, 56:60], smf[:1, 48:52],
                                                      0.5, None, ALU.mult))
        W("tensor", "ve", v_sc)
        P.wtok("tensor", l_cn)
        OP("tensor", lambda e: e.matmul(pc[:, 0:8], onr[:, :], smf[:1, 52:60],
                                        start=True, stop=True))
        OP("tensor", lambda e: e.matmul(pc[:, 8:24], onr[:, :], smf[:1, 0:16],
                                        start=True, stop=True))
        OP("tensor", lambda e: e.matmul(pc[:, 24:40], onr[:, :],
                                        smf[:1, 16:32], start=True, stop=True))
        pe_b = OP("tensor", lambda e: e.matmul(pc[:, 40:48], onr[:, :],
                                               smf[:1, 64:72], start=True,
                                               stop=True))
        W("vector", "pe", pe_b)
        OP("vector", lambda e: e.tensor_copy(gam[:, 0:8], pc[:, 0:8]))
        OP("vector", lambda e: e.tensor_scalar(ngm[:, 0:8], pc[:, 0:8], -1.0, None,
                                               ALU.mult))
        OP("vector", lambda e: e.tensor_scalar(srn[:, :], pc[:, 8:24], -1.0, None,
                                               ALU.mult))
        OP("vector", lambda e: e.tensor_copy(ecb[:, :], pc[:, 24:40]))
        v_bc = OP("vector", lambda e: e.tensor_copy(cvb[:, :], pc[:, 40:48]))

        l_qn = DIN("lx", rows[:1, 0:2048], qn_w)
        P.wtok("tensor", l_qn)
        W("tensor", "ve", v_bc)
        bcq_v = 0
        for i in range(4):
            p = OP("tensor", lambda e, i=i: e.matmul(
                pc[:, 0:512], onr[:, :], rows[:1, i * 512:(i + 1) * 512],
                start=True, stop=True))
            W("vector", "pe", p)
            bcq_v = OP("vector", lambda e, i=i: e.tensor_copy(
                bct[:, i * 512:(i + 1) * 512], pc[:, 0:512]))
            W("tensor", "ve", bcq_v)

        v_tab = 0
        a_tab = 0
        for h in range(H):
            hs = slice(h * 128, (h + 1) * 128)
            v1 = OP("vector", lambda e, hs=hs, h=h: e.tensor_scalar(
                masks[:, hs], cnst[:, 0:128], srn[:, h:h + 1], None, ALU.mult))
            W("scalar", "ve", v1)
            a1 = OP("scalar", lambda e, hs=hs: e.activation(masks[:, hs],
                                                            masks[:, hs], AF.Exp))
            W("vector", "ac", a1)
            OP("vector", lambda e, hs=hs: e.tensor_tensor(
                masks[:, hs], masks[:, hs], cnst[:, 128:256], ALU.mult))
            v2 = OP("vector", lambda e, hs=hs, h=h: e.tensor_scalar(
                lamc[:, hs], cnst[:, 256:384], srn[:, h:h + 1], None, ALU.mult))
            W("scalar", "ve", v2)
            a2 = OP("scalar", lambda e, hs=hs: e.activation(lamc[:, hs],
                                                            lamc[:, hs], AF.Exp))
            W("vector", "ac", a2)
            OP("vector", lambda e, hs=hs: e.tensor_scalar(
                lamc[:, hs], lamc[:, hs], DSCALE, None, ALU.mult))
            v3 = OP("vector", lambda e, h=h: e.tensor_scalar(
                lamr[:, h:h + 1], cnst[:, 384:385], srn[:, h:h + 1], None, ALU.mult))
            v4 = OP("vector", lambda e, h=h: e.tensor_scalar(
                lamT[:, h:h + 1], cnst[:, 385:386], srn[:, h:h + 1], None, ALU.mult))
            v5 = OP("vector", lambda e, h=h: e.tensor_scalar(
                segc[:, h * 8:(h + 1) * 8], ecb[:, 0:8], srn[:, h:h + 1], None,
                ALU.mult))
            W("scalar", "ve", v5)
            a3 = OP("scalar", lambda e, h=h: e.activation(
                lamr[:, h:h + 1], lamr[:, h:h + 1], AF.Exp))
            a4 = OP("scalar", lambda e, h=h: e.activation(
                lamT[:, h:h + 1], lamT[:, h:h + 1], AF.Exp))
            a_tab = OP("scalar", lambda e, h=h: e.activation(
                segc[:, h * 8:(h + 1) * 8], segc[:, h * 8:(h + 1) * 8], AF.Exp))
            W("vector", "ac", a_tab)
            v_tab = OP("vector", lambda e, h=h: e.tensor_tensor(
                segc[:, h * 8:(h + 1) * 8], segc[:, h * 8:(h + 1) * 8],
                ecb[:, 8:16], ALU.mult))
        v_w12 = OP("vector", lambda e: e.tensor_tensor(w12[:, :], nwc[:, :],
                                                       pwc[:, :], ALU.mult))

        P.enabled = (part == "A")
        ntile = nt // 128
        s1_evb_wd = [0]
        s1_sqb_wd = [0]
        P.barrier()
        s1_v = max(v_tab, v_w12, bcq_v)
        s1_a = a_tab
        prev_v = s1_v
        for i in range(ntile):
            W("sync", "ve", prev_v)
            lx = DIN("lx", scr32[:, 0:2048], x_tok[i * 128:(i + 1) * 128, :])
            xt = scr32[:, 0:2048]
            P.wtok("vector", lx)
            OP("vector", lambda e, xt=xt: e.tensor_reduce(sc1[:, 0:1], xt, AX,
                                                          ALU.add))
            P.wtok("scalar", lx)
            W("scalar", "ve", prev_v)
            P.wtok("scalar", s1_sqb_wd[0])
            aq = OP("scalar", lambda e, xt=xt: e.activation(sqb[:, 0:2048], xt,
                                                            AF.Square))
            W("vector", "ac", aq)
            OP("vector", lambda e: e.tensor_reduce(sc1[:, 1:2], sqb[:, 0:2048], AX,
                                                   ALU.add))
            OP("vector", lambda e: e.tensor_scalar(sc1[:, 2:3], sc1[:, 0:1],
                                                   1.0 / D, None, ALU.mult))
            OP("vector", lambda e: e.tensor_tensor(sc1[:, 3:4], sc1[:, 2:3],
                                                   sc1[:, 2:3], ALU.mult))
            OP("vector", lambda e: e.tensor_scalar(sc1[:, 4:5], sc1[:, 1:2],
                                                   1.0 / D, None, ALU.mult))
            v1 = OP("vector", lambda e: e.tensor_tensor(sc1[:, 4:5], sc1[:, 4:5],
                                                        sc1[:, 3:4], ALU.subtract))
            W("scalar", "ve", v1)
            OP("scalar", lambda e: e.activation(sc1[:, 5:6], sc1[:, 4:5], AF.Ln,
                                                bias=cvb[:, 0:1]))
            a2 = OP("scalar", lambda e: e.activation(sc1[:, 5:6], sc1[:, 5:6],
                                                     AF.Exp, scale=cvb[:, 3:4]))
            W("vector", "ac", a2)
            v_xn = OP("vector", lambda e, xt=xt: e.tensor_scalar(
                xt, xt, sc1[:, 2:3], sc1[:, 5:6], ALU.subtract, ALU.mult))
            W("scalar", "ve", v_xn)
            aq2 = OP("scalar", lambda e, xt=xt: e.activation(sqb[:, 0:2048], xt,
                                                             AF.Square))
            W("vector", "ac", aq2)
            OP("vector", lambda e: e.tensor_reduce(sc1[:, 6:7], sqb[:, 0:2048], AX,
                                                   ALU.add))
            v2 = OP("vector", lambda e: e.tensor_scalar(sc1[:, 7:8], sc1[:, 6:7],
                                                        1.0 / D, None, ALU.mult))
            W("scalar", "ve", v2)
            OP("scalar", lambda e: e.activation(sc1[:, 8:9], sc1[:, 7:8], AF.Ln,
                                                bias=cvb[:, 1:2]))
            a3 = OP("scalar", lambda e: e.activation(sc1[:, 8:9], sc1[:, 8:9],
                                                     AF.Exp, scale=cvb[:, 3:4]))
            OP("vector", lambda e, xt=xt: e.tensor_reduce(sc1[:, 9:10], xt, AX,
                                                          ALU.add))
            OP("vector", lambda e, xt=xt: e.tensor_reduce(sc1[:, 10:11], xt, AX,
                                                          ALU.max))
            OP("vector", lambda e, xt=xt: e.tensor_reduce(sc1[:, 11:12], xt, AX,
                                                          ALU.min))
            OP("vector", lambda e: e.tensor_scalar(sc1[:, 9:10], sc1[:, 9:10],
                                                   1.0 / D, None, ALU.mult))
            OP("vector", lambda e: e.tensor_tensor(sc1[:, 12:13], sc1[:, 9:10],
                                                   sc1[:, 9:10], ALU.mult))
            v4 = OP("vector", lambda e: e.tensor_tensor(sc1[:, 12:13], sc1[:, 7:8],
                                                        sc1[:, 12:13], ALU.subtract))
            W("scalar", "ve", v4)
            OP("scalar", lambda e: e.activation(sc1[:, 13:14], sc1[:, 12:13], AF.Ln,
                                                bias=cvb[:, 2:3]))
            a4 = OP("scalar", lambda e: e.activation(sc1[:, 13:14], sc1[:, 13:14],
                                                     AF.Exp, scale=cvb[:, 3:4]))
            W("vector", "ac", a4)
            OP("vector", lambda e: e.tensor_tensor(sc1[:, 10:11], sc1[:, 10:11],
                                                   sc1[:, 9:10], ALU.subtract))
            OP("vector", lambda e: e.tensor_tensor(sc1[:, 11:12], sc1[:, 9:10],
                                                   sc1[:, 11:12], ALU.subtract))
            OP("vector", lambda e: e.tensor_tensor(sc1[:, 10:11], sc1[:, 10:11],
                                                   sc1[:, 11:12], ALU.max))
            OP("vector", lambda e: e.reciprocal(sc1[:, 14:15], sc1[:, 10:11]))
            OP("vector", lambda e: e.tensor_scalar(sc1[:, 14:15], sc1[:, 14:15],
                                                   QB, None, ALU.mult))
            OP("vector", lambda e: e.tensor_tensor(sc1[:, 15:16], sc1[:, 10:11],
                                                   sc1[:, 13:14], ALU.mult))
            OP("vector", lambda e: e.tensor_scalar(sc1[:, 15:16], sc1[:, 15:16],
                                                   1.0 / QB, None, ALU.mult))
            OP("vector", lambda e, xt=xt: e.tensor_scalar(wf[:, 0:2048], xt,
                                                          sc1[:, 8:9], None,
                                                          ALU.mult))
            P.wtok("vector", s1_evb_wd[0])
            vh = OP("vector", lambda e: e.tensor_tensor(evb[:, 0:2048],
                                                        wf[:, 0:2048],
                                                        bct[:, 0:2048], ALU.mult))
            W("gpsimd", "ve", vh)
            s1_evb_wd[0] = DOUT("w1", h_attn_s[i * 128:(i + 1) * 128, :], evb[:, 0:2048])
            OP("vector", lambda e, xt=xt: e.tensor_scalar(
                wf[:, 0:2048], xt, sc1[:, 9:10], sc1[:, 14:15], ALU.subtract,
                ALU.mult))
            OP("vector", lambda e: e.tensor_scalar(
                wf[:, 0:2048], wf[:, 0:2048], QB - EPSF, -QB + EPSF, ALU.min,
                ALU.max))
            P.wtok("vector", s1_sqb_wd[0])
            vx = OP("vector", lambda e: e.tensor_scalar(
                sqb[:, 0:2048], wf[:, 0:2048], sc1[:, 15:16], None, ALU.mult))
            W("gpsimd", "ve", vx)
            s1_sqb_wd[0] = DOUT("w2", xh1_s[i * 128:(i + 1) * 128, :], sqb[:, 0:2048])
            prev_v = vx
        s1_w1 = ("w1", P.cnt["w1"]); s1_w2 = ("w2", P.cnt["w2"])

        tern_state = {"v": prev_v, "a": 0, "pe_war": 0}

        def tern(dst, wi, l_w, src, ncols):
            P.wtok("scalar", l_w)
            W("scalar", "ve", tern_state["v"])
            a1 = OP("scalar", lambda e, s=src, w=wi, n=ncols: e.activation(
                wsg[:, 0:n], s, AF.Sign, bias=ngm[:, w:w + 1]))
            a2 = OP("scalar", lambda e, s=src, w=wi, n=ncols: e.activation(
                wsg[:, 1024:1024 + n], s, AF.Sign, bias=gam[:, w:w + 1]))
            tern_state["a"] = a2
            W("vector", "ac", a2)
            W("vector", "pe", tern_state["pe_war"])
            v = OP("vector", lambda e, d=dst, n=ncols: e.tensor_tensor(
                d, wsg[:, 0:n], wsg[:, 1024:1024 + n], ALU.add))
            tern_state["v"] = v
            return v

        wf_par_a = [0, 0]
        pe_par = [0, 0]
        ev_par = [0, 0]
        ev_wd = [0, 0]
        P.barrier()
        for half in range(n_half):
            P.barrier()
            P.wtok("sync", s1_w1); P.wtok("sync", s1_w2)
            hl = 0
            for fc in range(16):
                hl = DIN("lb", big[:, fc * n_res:(fc + 1) * n_res],
                         h_attn_s[half * n_res:(half + 1) * n_res,
                                  fc * 128:(fc + 1) * 128], transpose=True)
            for os_ in range(8):
                tv = 0
                for fc in range(16):
                    par = fc % 2
                    W("sync", "ac", wf_par_a[par])
                    lw = DIN("lw0" if par == 0 else "lw1", wf[:, par * 1024:par * 1024 + 512], wkv_t[os_, fc, :, :])
                    tv = tern(wtb[:, fc * 512:(fc + 1) * 512], 0, lw,
                              wf[:, par * 1024:par * 1024 + 512], 512)
                    wf_par_a[par] = tern_state["a"]
                W("tensor", "ve", tv)
                P.wtok("tensor", hl)
                for tcl in range(ntc_res):
                    par = tcl % 2
                    pp = pa if par == 0 else pb
                    W("tensor", "ve", ev_par[par])
                    pe_l = 0
                    for fc in range(16):
                        pe_l = OP("tensor", lambda e, fc=fc, t=tcl, pp=pp: e.matmul(
                            pp[:, 0:512],
                            big[:, fc * n_res + t * 128:fc * n_res + (t + 1) * 128],
                            wtb[:, fc * 512:(fc + 1) * 512],
                            start=(fc == 0), stop=(fc == 15)))
                    pe_par[par] = pe_l
                    tern_state["pe_war"] = pe_l
                    W("vector", "pe", pe_l)
                    P.wtok("vector", ev_wd[par])
                    ev = OP("vector", lambda e, par=par, pp=pp: e.tensor_scalar(
                        ev2[:, par * 1024:par * 1024 + 512], pp[:, 0:512],
                        gam[:, 4:5], None, ALU.mult))
                    ev_par[par] = ev
                    tglob = half * ntc_res + tcl
                    dst = k_s if os_ < 4 else v_s
                    od = (os_ % 4) * 512
                    W("gpsimd", "ve", ev)
                    ev_wd[par] = DOUT("we0" if par == 0 else "we1",
                         dst[tglob * 128:(tglob + 1) * 128, od:od + 512],
                         ev2[:, par * 1024:par * 1024 + 512])
            for oc in range(16):
                W("sync", "ac", tern_state["a"])
                lw = DIN("lw0", wf[:, 0:2048].rearrange("p (f c) -> p f c", c=128),
                         wq_t[oc, :, :, :].rearrange("f p c -> p f c"))
                tv = 0
                for r in range(2):
                    tv = tern(wtb[:, r * 1024:(r + 1) * 1024], 0, lw,
                              wf[:, r * 1024:(r + 1) * 1024], 1024)
                W("tensor", "ve", tv)
                for kl in range(n_kilo):
                    par = kl % 2
                    pp = pa if par == 0 else pb
                    W("tensor", "ve", ev_par[par])
                    pe_l = 0
                    for fc in range(16):
                        for hv in range(2):
                            pe_l = OP("tensor", lambda e, fc=fc, k=kl, pp=pp,
                                      hv=hv: e.matmul(
                                pp[:, hv * 512:(hv + 1) * 512],
                                wtb[:, fc * 128:(fc + 1) * 128],
                                big[:, fc * n_res + k * 1024 + hv * 512:
                                    fc * n_res + k * 1024 + (hv + 1) * 512],
                                start=(fc == 0), stop=(fc == 15)))
                    tern_state["pe_war"] = pe_l
                    W("vector", "pe", pe_l)
                    P.wtok("vector", ev_wd[par])
                    ev = OP("vector", lambda e, par=par, pp=pp: e.tensor_scalar(
                        ev2[:, par * 1024:(par + 1) * 1024], pp[:, 0:1024],
                        gam[:, 4:5], None, ALU.mult))
                    ev_par[par] = ev
                    W("gpsimd", "ve", ev)
                    ev_wd[par] = DOUT("we0" if par == 0 else "we1",
                             q_s[oc, :, half * n_res + kl * 1024:
                             half * n_res + (kl + 1) * 1024],
                         ev2[:, par * 1024:(par + 1) * 1024])
                wf_par_a[0] = tern_state["a"]
                wf_par_a[1] = tern_state["a"]
        s2_done = [("we0", P.cnt["we0"]), ("we1", P.cnt["we1"])]

        GRP = 4
        ngrp = nch // GRP
        P.barrier()
        P.wtok("sync", s2_done[0]); P.wtok("sync", s2_done[1])
        kv_par_use = [0, 0]
        kv_par_pe = [0, 0]
        pkv_consumed = 0
        for h in range(H):
            hs = slice(h * 128, (h + 1) * 128)
            zv = OP("vector", lambda e, hs=hs: e.memset(Sf[:, hs], 0.0))
            for g in range(ngrp):
                par = g % 2
                W("sync", "ve", kv_par_use[par])
                W("sync", "pe", kv_par_pe[par])
                sst = "ls0" if par == 0 else "ls1"
                lk = DIN(sst, kt2[:, par * 512:(par + 1) * 512]
                         .rearrange("p (g d) -> p g d", d=128),
                         k4d[g * GRP:(g + 1) * GRP, h].rearrange("g p d -> p g d"))
                lv = DIN(sst, vt2[:, par * 512:(par + 1) * 512]
                         .rearrange("p (g d) -> p g d", d=128),
                         v4d[g * GRP:(g + 1) * GRP, h].rearrange("g p d -> p g d"))
                P.wtok("vector", lv)
                P.wtok("tensor", lv)
                for c in range(GRP):
                    co = par * 512 + c * 128
                    vk = OP("vector", lambda e, co=co, h=h: e.tensor_scalar(
                        ktb[:, :], kt2[:, co:co + 128], lamr[:, h:h + 1], None,
                        ALU.mult))
                    W("tensor", "ve", vk)
                    W("tensor", "ve", pkv_consumed)
                    pk = OP("tensor", lambda e, co=co: e.matmul(
                        pb[:, 0:128], ktb[:, :], vt2[:, co:co + 128],
                        start=True, stop=True))
                    OP("vector", lambda e, hs=hs, h=h: e.tensor_scalar(
                        Sf[:, hs], Sf[:, hs], lamT[:, h:h + 1], None, ALU.mult))
                    W("vector", "pe", pk)
                    pkv_consumed = OP("vector", lambda e, hs=hs: e.tensor_tensor(
                        Sf[:, hs], Sf[:, hs], pb[:, 0:128], ALU.add))
                    kv_par_pe[par] = P.cnt["pe"]
                kv_par_use[par] = pkv_consumed
            W("gpsimd", "ve", pkv_consumed)
            d_cc = DOUT("wc", cc_in[h * 128:(h + 1) * 128, :], Sf[:, hs])
        P.enabled = (part == "B")

        sg_par = [0, 0]
        po_sb = 0
        for h in range(H):
            hs = slice(h * 128, (h + 1) * 128)
            for r in range(NCORE):
                par = r % 2
                W("sync", "ve", sg_par[par])
                lr = DIN("lg0" if par == 0 else "lg1",
                         sgt[:, par * 128:(par + 1) * 128],
                         cc_out[(r * H + h) * 128:(r * H + h + 1) * 128, :])
                P.wtok("vector", lr)
                if r == 0:
                    sg_par[par] = OP("vector", lambda e, hs=hs, h=h, par=par:
                                     e.tensor_scalar(Sf[:, hs],
                                                     sgt[:, par * 128:(par + 1) * 128],
                                                     segc[:, h * 8:h * 8 + 1], None,
                                                     ALU.mult))
                else:
                    sg_par[par] = OP("vector", lambda e, hs=hs, h=h, r=r, par=par:
                                     e.scalar_tensor_tensor(
                                         Sf[:, hs],
                                         sgt[:, par * 128:(par + 1) * 128],
                                         segc[:, h * 8 + r:h * 8 + r + 1],
                                         Sf[:, hs], ALU.mult, ALU.add))
            W("gpsimd", "ve", sg_par[(NCORE - 1) % 2])
            po_sb = OP("gpsimd", lambda e, hs=hs: e.tensor_copy(Sb[:, hs],
                                                                Sf[:, hs]))

        P.barrier()
        st_par_use = [0, 0]
        st_par_pe = [0, 0]
        och_par_wd = [0, 0]
        och_par_v = [0, 0]
        pe_cross = 0
        pe_kv = 0
        v_oc = 0
        v_sadd = 0
        po_prev_sb = po_sb
        for h in range(H):
            hs = slice(h * 128, (h + 1) * 128)
            for g in range(ngrp):
                par = g % 2
                W("sync", "ve", st_par_use[par])
                W("sync", "pe", st_par_pe[par])
                P.wtok("vector", och_par_wd[par])
                sst = "ls0" if par == 0 else "ls1"
                lq = DIN(sst, qg2[:, par * 512:(par + 1) * 512],
                         q_s[h, :, g * 512:(g + 1) * 512])
                lkf = DIN(sst, kf2[:, par * 512:(par + 1) * 512],
                          k_s[g * 512:(g + 1) * 512, h * 128:(h + 1) * 128],
                          transpose=True)
                lk = DIN(sst, kt2[:, par * 512:(par + 1) * 512]
                         .rearrange("p (g d) -> p g d", d=128),
                         k4d[g * GRP:(g + 1) * GRP, h].rearrange("g p d -> p g d"))
                lv = DIN(sst, vt2[:, par * 512:(par + 1) * 512]
                         .rearrange("p (g d) -> p g d", d=128),
                         v4d[g * GRP:(g + 1) * GRP, h].rearrange("g p d -> p g d"))
                P.wtok("vector", lv)
                P.wtok("tensor", lv)
                for c in range(GRP):
                    co = par * 512 + c * 128
                    cglob = g * GRP + c
                    W("tensor", "ve", v_oc)
                    psc = OP("tensor", lambda e, co=co: e.matmul(
                        pa[:, 0:128], kf2[:, co:co + 128], qg2[:, co:co + 128],
                        start=True, stop=True))
                    v_qt = OP("vector", lambda e, co=co, h=h: e.tensor_tensor(
                        qtb[:, :], qg2[:, co:co + 128], lamc[:, h * 128:(h + 1) * 128],
                        ALU.mult))
                    W("vector", "pe", psc)
                    v_st = OP("vector", lambda e, h=h: e.tensor_tensor(
                        stb[:, :], pa[:, 0:128], masks[:, h * 128:(h + 1) * 128],
                        ALU.mult))
                    W("tensor", "ve", v_st)
                    OP("tensor", lambda e, co=co: e.matmul(
                        pa[:, 512:640], vt2[:, co:co + 128], stb[:, :],
                        start=True, stop=False))
                    W("tensor", "po", po_prev_sb)
                    pe_cross = OP("tensor", lambda e, hs=hs: e.matmul(
                        pa[:, 512:640], Sb[:, hs], qtb[:, :],
                        start=False, stop=True))
                    W("vector", "pe", pe_cross)
                    v_oc = OP("vector", lambda e, par=par, c=c: e.tensor_copy(
                        och[:, par * 512 + c * 128:par * 512 + (c + 1) * 128],
                        pa[:, 512:640]))
                    och_par_v[par] = v_oc
                    vk = OP("vector", lambda e, co=co, h=h: e.tensor_scalar(
                        ktb[:, :], kt2[:, co:co + 128], lamr[:, h:h + 1], None,
                        ALU.mult))
                    W("tensor", "ve", vk)
                    W("tensor", "ve", v_sadd)
                    pe_kv = OP("tensor", lambda e, co=co: e.matmul(
                        pb[:, 0:128], ktb[:, :], vt2[:, co:co + 128],
                        start=True, stop=True))
                    W("gpsimd", "pe", pe_cross)
                    po_scale = OP("gpsimd", lambda e, hs=hs, h=h: e.tensor_scalar(
                        Sf[:, hs], Sf[:, hs], lamT[:, h:h + 1], None, ALU.mult))
                    W("vector", "pe", pe_kv)
                    W("vector", "po", po_scale)
                    v_sadd = OP("vector", lambda e, hs=hs: e.tensor_tensor(
                        Sf[:, hs], Sf[:, hs], pb[:, 0:128], ALU.add))
                    W("gpsimd", "ve", v_sadd)
                    po_prev_sb = OP("gpsimd", lambda e, hs=hs: e.tensor_copy(
                        Sb[:, hs], Sf[:, hs]))
                W("gpsimd", "ve", och_par_v[par])
                och_par_wd[par] = DOUT("wo0" if par == 0 else "wo1",
                                       attno_s[h, :, g * 512:(g + 1) * 512],
                                       och[:, par * 512:(par + 1) * 512])
                st_par_use[par] = v_oc
                st_par_pe[par] = P.cnt["pe"]
            if h < H - 1:
                pass
        s3_done = None

        cf_state = {"wr": None, "lr": None}
        cf_state = {"wr": None, "lr": None}

        for half in range(n_half):
            P.barrier()
            pass
            lslab = None
            for oc in range(16):
                lslab = DIN("lb", big[:, oc * n_res:(oc + 1) * n_res],
                            attno_s[oc, :, half * n_res:(half + 1) * n_res])
            P.wtok("scalar", lslab)
            P.wtok("tensor", lslab)
            mm_prev = 0
            for sl in range(n_res // 512):
                for st_i in range(2):
                    mm = 0
                    for oc in range(16):
                        W("scalar", "pe", mm_prev)
                        if st_i == 0:
                            sq_a = OP("scalar", lambda e, oc=oc, sl=sl: e.activation(
                                sqb[:, 0:512],
                                big[:, oc * n_res + sl * 512:oc * n_res + (sl + 1) * 512],
                                AF.Square))
                        else:
                            sq_a = OP("scalar", lambda e, oc=oc, sl=sl: e.activation(
                                sqb[:, 0:512],
                                big[:, oc * n_res + sl * 512:oc * n_res + (sl + 1) * 512],
                                AF.Square, scale=nwc[:, oc:oc + 1]))
                        W("tensor", "ac", sq_a)
                        mm = OP("tensor", lambda e, oc=oc, st_i=st_i: e.matmul(
                            (pt1 if st_i == 0 else pt2)[:1, 0:512], onb[:, :],
                            sqb[:, 0:512], start=(oc == 0), stop=(oc == 15)))
                        mm_prev = mm
                    W("vector", "pe", mm)
                    vcopy = OP("vector", lambda e, st_i=st_i: e.tensor_copy(
                        rowt2[:1, st_i * 512:(st_i + 1) * 512],
                        (pt1 if st_i == 0 else pt2)[:1, 0:512]))
                W("scalar", "ve", vcopy)
                a1_ = OP("scalar", lambda e: e.activation(
                    rowt2[:1, 0:512], rowt2[:1, 0:512], AF.Ln,
                    scale=cvb[0:1, 4:5], bias=cvb[0:1, 2:3]))
                a2_ = OP("scalar", lambda e: e.activation(
                    rowt2[:1, 0:512], rowt2[:1, 0:512], AF.Exp,
                    scale=cvb[0:1, 3:4]))
                W("vector", "ac", a2_)
                OP("vector", lambda e: e.tensor_tensor(
                    rowt2[:1, 1024:1536], rowt2[:1, 0:512], rowt2[:1, 0:512],
                    ALU.mult))
                v_t = OP("vector", lambda e: e.tensor_tensor(
                    rowt2[:1, 1024:1536], rowt2[:1, 1024:1536],
                    rowt2[:1, 512:1024], ALU.mult))
                W("scalar", "ve", v_t)
                OP("scalar", lambda e: e.activation(
                    rowt2[:1, 1024:1536], rowt2[:1, 1024:1536], AF.Ln,
                    scale=cvb[0:1, 4:5], bias=cvb[0:1, 1:2]))
                a3_ = OP("scalar", lambda e: e.activation(
                    rowt2[:1, 1024:1536], rowt2[:1, 1024:1536], AF.Exp,
                    scale=cvb[0:1, 3:4]))
                W("vector", "ac", a3_)
                OP("vector", lambda e, sl=sl: e.tensor_tensor(
                    rows[:1, sl * 512:(sl + 1) * 512], rowt2[:1, 0:512],
                    rowt2[:1, 1024:1536], ALU.mult))
            bcv = 0
            for i in range(n_res // 512):
                vpre = OP("vector", lambda e, i=i: e.tensor_copy(
                    smf[:1, 80:81], rows[:1, i * 512:i * 512 + 1]))
                W("tensor", "ve", vpre)
                p = OP("tensor", lambda e, i=i: e.matmul(
                    pc[:, 0:512], onr[:, :], rows[:1, i * 512:(i + 1) * 512],
                    start=True, stop=True))
                W("vector", "pe", p)
                bcv = OP("vector", lambda e, i=i: e.tensor_copy(
                    bct[:, i * 512:(i + 1) * 512], pc[:, 0:512]))
                W("tensor", "ve", bcv)
            W("vector", "ve", bcv)
            zv = 0
            for oc in range(16):
                bslab = big[:, oc * n_res:(oc + 1) * n_res]
                OP("vector", lambda e, b=bslab: e.tensor_tensor(
                    b, b, bct[:, 0:n_res], ALU.mult))
                zv = OP("vector", lambda e, b=bslab, oc=oc: e.tensor_scalar(
                    b, b, w12[:, oc:oc + 1], None, ALU.mult))
            W("tensor", "ve", zv)
            for oc in range(16):
                W("sync", "ac", tern_state["a"])
                W("sync", "ve", tern_state["v"])
                lw = DIN("lw0", wf[:, 0:2048].rearrange("p (f c) -> p f c", c=128),
                         wp_t[oc, :, :, :].rearrange("f p c -> p f c"))
                tv = 0
                for r in range(2):
                    tv = tern(wtb[:, r * 1024:(r + 1) * 1024], 1, lw,
                              wf[:, r * 1024:(r + 1) * 1024], 1024)
                W("tensor", "ve", tv)
                for kl in range(n_kilo):
                    par = kl % 2
                    pp = pa if par == 0 else pb
                    W("tensor", "ve", ev_par[par])
                    pe_l = 0
                    for fc in range(16):
                        for hv in range(2):
                            pe_l = OP("tensor", lambda e, fc=fc, k=kl, pp=pp,
                                      hv=hv: e.matmul(
                                pp[:, hv * 512:(hv + 1) * 512],
                                wtb[:, fc * 128:(fc + 1) * 128],
                                big[:, fc * n_res + k * 1024 + hv * 512:
                                    fc * n_res + k * 1024 + (hv + 1) * 512],
                                start=(fc == 0), stop=(fc == 15)))
                    tern_state["pe_war"] = pe_l
                    W("vector", "pe", pe_l)
                    P.wtok("vector", ev_wd[par])
                    ev = OP("vector", lambda e, par=par, pp=pp: e.tensor_scalar(
                        ev2[:, par * 1024:(par + 1) * 1024], pp[:, 0:1024],
                        gam[:, 5:6], None, ALU.mult))
                    ev_par[par] = ev
                    W("gpsimd", "ve", ev)
                    ev_wd[par] = DOUT("we0" if par == 0 else "we1",
                             s_s[oc, :, half * n_res + kl * 1024:
                             half * n_res + (kl + 1) * 1024],
                         ev2[:, par * 1024:(par + 1) * 1024])
        

        wpf_flat_in = wpf_t.rearrange("a b p q -> (a b) p q")
        wpf_flat_out = wpf_b.rearrange("a b p q -> (a b) p q")
        n_wpf_chunks = 128
        wpf_i = 0
        gel_a = 0
        ev5_v = 0
        wpf_wd = [0]
        hs_wd = [0]
        for half in range(n_half):
            P.barrier()
            pass
            for fc in range(16):
                hl = DIN("lb", big[:, fc * n_res:(fc + 1) * n_res],
                         xh1_s[half * n_res:(half + 1) * n_res,
                               fc * 128:(fc + 1) * 128], transpose=True)
            for oc in range(64):
                W("sync", "ac", tern_state["a"])
                W("sync", "ve", tern_state["v"])
                lw = DIN("lw0", wf[:, 0:2048].rearrange("p (f c) -> p f c", c=128),
                         wfc_t[oc, :, :, :].rearrange("f p c -> p f c"))
                tv = 0
                for r in range(2):
                    tv = tern(wtb[:, r * 1024:(r + 1) * 1024], 2, lw,
                              wf[:, r * 1024:(r + 1) * 1024], 1024)
                W("tensor", "ve", tv)
                P.wtok("tensor", hl)
                for kl in range(n_kilo):
                    par = kl % 2
                    pp = pa if par == 0 else pb
                    W("tensor", "ac", gel_a)
                    pe_l = 0
                    for fc in range(16):
                        for hv in range(2):
                            pe_l = OP("tensor", lambda e, fc=fc, k=kl, pp=pp,
                                      hv=hv: e.matmul(
                                pp[:, hv * 512:(hv + 1) * 512],
                                wtb[:, fc * 128:(fc + 1) * 128],
                                big[:, fc * n_res + k * 1024 + hv * 512:
                                    fc * n_res + k * 1024 + (hv + 1) * 512],
                                start=(fc == 0), stop=(fc == 15)))
                    tern_state["pe_war"] = pe_l
                    W("vector", "pe", pe_l)
                    W("vector", "ac", gel_a)
                    ev = OP("vector", lambda e, pp=pp, k=kl: e.tensor_scalar(
                        scr32[:, k * 1024:(k + 1) * 1024] if n_kilo > 1 else
                        scr32[:, 0:1024], pp[:, 0:1024], gam[:, 6:7], None,
                        ALU.mult))
                    ev5_v = ev
                    W("scalar", "ve", ev)
                    P.wtok("scalar", hs_wd[0])
                    gel_a = OP("scalar", lambda e, k=kl: e.activation(
                        evb[:, k * 1024:(k + 1) * 1024] if n_kilo > 1 else
                        evb[:, 0:1024],
                        scr32[:, k * 1024:(k + 1) * 1024] if n_kilo > 1 else
                        scr32[:, 0:1024], AF.Gelu))
                    W("tensor", "ve", ev)
                W("gpsimd", "ac", gel_a)
                hs_wd[0] = DOUT("w5", h_s[oc, :, half * n_res:(half + 1) * n_res],
                     evb[:, 0:n_res])
                if wpf_i < n_wpf_chunks:
                    j = wpf_i
                    wpf_i += 1
                    W("sync", "ac", tern_state["a"])
                    lwp = DIN("lw1", wf[:, 0:1024].rearrange("p (f c) -> p f c", c=128),
                              wpf_flat_in[j * 8:(j + 1) * 8].rearrange("f p c -> p f c"))
                    P.wtok("vector", wpf_wd[0])
                    tvp = tern(sqb[:, 0:1024], 3, lwp, wf[:, 0:1024], 1024)
                    W("gpsimd", "ve", tvp)
                    wpf_wd[0] = DOUT("wp5", wpf_flat_out[j * 8:(j + 1) * 8].rearrange("f p c -> p f c"),
                         sqb[:, 0:1024].rearrange("p (f c) -> p f c", c=128))
        while wpf_i < n_wpf_chunks:
            j = wpf_i
            wpf_i += 1
            W("sync", "ac", tern_state["a"])
            lwp = DIN("lw1", wf[:, 0:1024].rearrange("p (f c) -> p f c", c=128),
                      wpf_flat_in[j * 8:(j + 1) * 8].rearrange("f p c -> p f c"))
            P.wtok("vector", wpf_wd[0])
            tvp = tern(sqb[:, 0:1024], 3, lwp, wf[:, 0:1024], 1024)
            W("gpsimd", "ve", tvp)
            wpf_wd[0] = DOUT("wp5", wpf_flat_out[j * 8:(j + 1) * 8].rearrange("f p c -> p f c"),
                 sqb[:, 0:1024].rearrange("p (f c) -> p f c", c=128))
        

        P.barrier()
        hs_par_v = [0, 0]
        hs_par_ac = [0, 0]
        hs_par_pe = [0, 0]
        fin_par = [0, 0]
        fob_wd = [0, 0]
        sx_par_v = [0, 0]
        wtb_pe = [0]
        for g in range(ng):
            P.barrier()
            g0 = g * 512
            az = OP("vector", lambda e: e.memset(macc[:, 0:16], -3e38))
            az2 = OP("vector", lambda e: e.memset(macc[:, 16:32], 3e38))
            mm_prev = 0
            mm1 = mm2 = 0
            tp_v = 0
            for oc in range(64):
                par = oc % 2
                W("sync", "ve", hs_par_v[par])
                W("sync", "ac", hs_par_ac[par])
                W("sync", "pe", hs_par_pe[par])
                lh = DIN("lh60" if par == 0 else "lh61",
                         evb[:, par * 1024:par * 1024 + 512],
                         h_s[oc, :, g0:g0 + 512])
                hsl = evb[:, par * 1024:par * 1024 + 512]
                P.wtok("scalar", lh)
                W("scalar", "pe", mm_prev)
                sqa = OP("scalar", lambda e, h=hsl: e.activation(
                    sqb[:, 0:512], h, AF.Square))
                hs_par_ac[par] = sqa
                P.wtok("tensor", lh)
                mm1 = OP("tensor", lambda e, oc=oc, h=hsl: e.matmul(
                    pt1[:1, 0:512], onb[:, :], h, start=(oc == 0),
                    stop=(oc == 63)))
                W("tensor", "ac", sqa)
                mm2 = OP("tensor", lambda e, oc=oc: e.matmul(
                    pt2[:1, 0:512], onb[:, :], sqb[:, 0:512], start=(oc == 0),
                    stop=(oc == 63)))
                mm_prev = mm2
                hs_par_pe[par] = mm2
                P.wtok("vector", lh)
                if oc == 0:
                    W("vector", "ve", az2)
                tp_v = OP("vector", lambda e, h=hsl: e.transpose(
                    sqb[:, 1024:1536], h))
                OP("vector", lambda e: e.tensor_reduce(
                    macc[:, 32:48],
                    sqb[:, 1024:1536].rearrange("p (j q) -> p j q", q=32), AX,
                    ALU.max))
                OP("vector", lambda e: e.tensor_tensor(
                    macc[:, 0:16], macc[:, 0:16], macc[:, 32:48], ALU.max))
                OP("vector", lambda e: e.tensor_reduce(
                    macc[:, 32:48],
                    sqb[:, 1024:1536].rearrange("p (j q) -> p j q", q=32), AX,
                    ALU.min))
                hs_par_v[par] = OP("vector", lambda e: e.tensor_tensor(
                    macc[:, 16:32], macc[:, 16:32], macc[:, 32:48], ALU.min))
            if cf_state["lr"]:
                P.wtok("gpsimd", cf_state["lr"])
            W("gpsimd", "ve", hs_par_v[0] if hs_par_v[0] > hs_par_v[1]
              else hs_par_v[1])
            dw = DOUT("wr", stat_d[:, 0:32], macc[:, 0:32])
            cf_state["wr"] = dw
            P.wtok("sync", dw)
            lr = DIN("lr", rows[:1, 0:2048]
                     .rearrange("a (b pp j) -> a b pp j", b=4, pp=32),
                     stat_d[:, 0:16].rearrange("(a b pp) j -> a b pp j",
                                               a=1, pp=32))
            lr2 = DIN("lr", rowt2[:1, 0:2048]
                      .rearrange("a (b pp j) -> a b pp j", b=4, pp=32),
                      stat_d[:, 16:32].rearrange("(a b pp) j -> a b pp j",
                                                 a=1, pp=32))
            cf_state["lr"] = lr2
            P.wtok("vector", lr2)
            OP("vector", lambda e: e.tensor_tensor(
                rows[:1, 0:512], rows[:1, 0:512], rows[:1, 512:1024], ALU.max))
            OP("vector", lambda e: e.tensor_tensor(
                rows[:1, 0:512], rows[:1, 0:512], rows[:1, 1024:1536], ALU.max))
            OP("vector", lambda e: e.tensor_tensor(
                rows[:1, 0:512], rows[:1, 0:512], rows[:1, 1536:2048], ALU.max))
            OP("vector", lambda e: e.tensor_tensor(
                rowt2[:1, 0:512], rowt2[:1, 0:512], rowt2[:1, 512:1024],
                ALU.min))
            OP("vector", lambda e: e.tensor_tensor(
                rowt2[:1, 0:512], rowt2[:1, 0:512], rowt2[:1, 1024:1536],
                ALU.min))
            vmm = OP("vector", lambda e: e.tensor_tensor(
                rowt2[:1, 0:512], rowt2[:1, 0:512], rowt2[:1, 1536:2048],
                ALU.min))
            W("vector", "pe", mm2)
            OP("vector", lambda e: e.tensor_copy(rows[:1, 512:1024],
                                                 pt1[:1, 0:512]))
            OP("vector", lambda e: e.tensor_copy(rowt2[:1, 512:1024],
                                                 pt2[:1, 0:512]))
            OP("vector", lambda e: e.tensor_scalar(
                rows[:1, 1024:1536], rows[:1, 512:1024], 1.0 / DFF, None,
                ALU.mult))
            OP("vector", lambda e: e.tensor_scalar(
                rowt2[:1, 512:1024], rowt2[:1, 512:1024], 1.0 / DFF, None,
                ALU.mult))
            OP("vector", lambda e: e.tensor_tensor(
                rowt2[:1, 1536:2048], rows[:1, 1024:1536], rows[:1, 1024:1536],
                ALU.mult))
            v_var = OP("vector", lambda e: e.tensor_tensor(
                rowt2[:1, 1024:1536], rowt2[:1, 512:1024],
                rowt2[:1, 1536:2048], ALU.subtract))
            W("scalar", "ve", v_var)
            OP("scalar", lambda e: e.activation(
                rowt2[:1, 1024:1536], rowt2[:1, 1024:1536], AF.Ln,
                bias=cvb[0:1, 2:3]))
            a_rs = OP("scalar", lambda e: e.activation(
                rowt2[:1, 1024:1536], rowt2[:1, 1024:1536], AF.Exp,
                scale=cvb[0:1, 3:4]))
            W("vector", "ac", a_rs)
            OP("vector", lambda e: e.tensor_tensor(
                rows[:1, 0:512], rows[:1, 0:512], rows[:1, 1024:1536],
                ALU.subtract))
            OP("vector", lambda e: e.tensor_tensor(
                rowt2[:1, 0:512], rows[:1, 1024:1536], rowt2[:1, 0:512],
                ALU.subtract))
            OP("vector", lambda e: e.tensor_tensor(
                rows[:1, 0:512], rows[:1, 0:512], rowt2[:1, 0:512], ALU.max))
            OP("vector", lambda e: e.reciprocal(rows[:1, 512:1024],
                                                rows[:1, 0:512]))
            OP("vector", lambda e: e.tensor_scalar(
                rows[:1, 512:1024], rows[:1, 512:1024], QB, None, ALU.mult))
            OP("vector", lambda e: e.tensor_tensor(
                rowt2[:1, 0:512], rows[:1, 0:512], rowt2[:1, 1024:1536],
                ALU.mult))
            OP("vector", lambda e: e.tensor_scalar(
                rowt2[:1, 0:512], rowt2[:1, 0:512], 1.0 / QB, None, ALU.mult))
            vfin = OP("vector", lambda e: e.scalar_tensor_tensor(
                rows[:1, 1536:2048], rows[:1, 1024:1536], -1.0,
                rows[:1, 512:1024], ALU.mult, ALU.mult))
            W("tensor", "ve", vfin)
            bcv = 0
            for i, rsrc in enumerate((rows[:1, 512:1024], rows[:1, 1536:2048],
                                      rowt2[:1, 0:512])):
                p = OP("tensor", lambda e, r=rsrc: e.matmul(
                    pc[:, 0:512], onr[:, :], r, start=True, stop=True))
                W("vector", "pe", p)
                bcv = OP("vector", lambda e, i=i: e.tensor_copy(
                    bct[:, i * 512:(i + 1) * 512], pc[:, 0:512]))
                W("tensor", "ve", bcv)
            xh_v = 0
            for oc in range(64):
                par = oc % 2
                W("sync", "ve", hs_par_v[par])
                W("sync", "ac", hs_par_ac[par])
                W("sync", "pe", hs_par_pe[par])
                lh = DIN("lh60" if par == 0 else "lh61",
                         evb[:, par * 1024:par * 1024 + 512],
                         h_s[oc, :, g0:g0 + 512])
                hsl = evb[:, par * 1024:par * 1024 + 512]
                P.wtok("vector", lh)
                if oc < 2:
                    W("vector", "ve", bcv)
                OP("vector", lambda e, h=hsl: e.tensor_tensor(
                    scr32[:, 0:512], h, bct[:, 0:512], ALU.mult))
                OP("vector", lambda e: e.tensor_tensor(
                    scr32[:, 0:512], scr32[:, 0:512], bct[:, 512:1024], ALU.add))
                OP("vector", lambda e: e.tensor_scalar(
                    scr32[:, 0:512], scr32[:, 0:512], QB - EPSF, -QB + EPSF,
                    ALU.min, ALU.max))
                xh_v = OP("vector", lambda e, oc=oc: e.tensor_tensor(
                    big[:, oc * 512:(oc + 1) * 512], scr32[:, 0:512],
                    bct[:, 1024:1536], ALU.mult))
                hs_par_v[par] = xh_v
            W("tensor", "ve", xh_v)
            for oc in range(16):
                par = oc % 2
                W("sync", "pe", wtb_pe[0])
                W("sync", "ve", tern_state["v"])
                lw = DIN("lw0", wtb[:, 0:8192].rearrange("p (f c) -> p f c", c=128),
                         wpf_b[oc].rearrange("f p c -> p f c"))
                P.wtok("tensor", lw)
                pp = pa if par == 0 else pb
                W("tensor", "ve", ev_par[par])
                pe_l = 0
                for fc in range(64):
                    pe_l = OP("tensor", lambda e, fc=fc, pp=pp: e.matmul(
                        pp[:, 0:512], wtb[:, fc * 128:(fc + 1) * 128],
                        big[:, fc * 512:(fc + 1) * 512],
                        start=(fc == 0), stop=(fc == 63)))
                pe_par[par] = pe_l
                wtb_pe[0] = pe_l
                W("sync", "ve", sx_par_v[par])
                ls = DIN("lf0" if par == 0 else "lf1", ssb[:, par * 512:(par + 1) * 512], s_s[oc, :, g0:g0 + 512])
                lx2 = DIN("lf0" if par == 0 else "lf1", xfb[:, par * 512:(par + 1) * 512], x_fm[oc, :, g0:g0 + 512])
                W("vector", "pe", pe_l)
                W("vector", "ac", fin_par[par])
                ev = OP("vector", lambda e, pp=pp, par=par: e.tensor_scalar(
                    scr32[:, 1024 + par * 512:1024 + (par + 1) * 512], pp[:, 0:512],
                    gam[:, 7:8], None, ALU.mult))
                ev_par[par] = ev
                W("scalar", "ve", ev)
                asl = OP("scalar", lambda e, par=par: e.activation(
                    scr32[:, 1024 + par * 512:1024 + (par + 1) * 512],
                    scr32[:, 1024 + par * 512:1024 + (par + 1) * 512], AF.Silu))
                fin_par[par] = asl
                W("vector", "ac", asl)
                P.wtok("vector", lx2)
                P.wtok("vector", fob_wd[par])
                OP("vector", lambda e, par=par: e.tensor_tensor(
                    fob[:, par * 512:(par + 1) * 512],
                    scr32[:, 1024 + par * 512:1024 + (par + 1) * 512],
                    ssb[:, par * 512:(par + 1) * 512], ALU.mult))
                fv = OP("vector", lambda e, par=par: e.tensor_tensor(
                    fob[:, par * 512:(par + 1) * 512],
                    fob[:, par * 512:(par + 1) * 512],
                    xfb[:, par * 512:(par + 1) * 512], ALU.add))
                sx_par_v[par] = fv
                W("gpsimd", "ve", fv)
                fob_wd[par] = DOUT("wf0" if par == 0 else "wf1",
                                   out_fm[oc, :, g0:g0 + 512],
                                   fob[:, par * 512:(par + 1) * 512])
        P.enabled = True
        P.wait_all_dma("gpsimd", fam="w")

        with nc.Block() as block:
            @block.sync
            def _(e):
                for fn in P.ops["sync"]:
                    fn(e)

            @block.gpsimd
            def _(e):
                for fn in P.ops["gpsimd"]:
                    fn(e)

            @block.vector
            def _(e):
                for fn in P.ops["vector"]:
                    fn(e)

            @block.scalar
            def _(e):
                for fn in P.ops["scalar"]:
                    fn(e)

            @block.tensor
            def _(e):
                for fn in P.ops["tensor"]:
                    fn(e)

    return nc


def make_inputs(x, qkv_w, qkv_norm_w, proj_w, proj_norm_w, norm_w, s, w_fc,
                w_proj, nt):
    def lhsT_tiles(w, noc, nfc):
        wt = np.ascontiguousarray(w.T)
        t = wt.reshape(nfc, 128, noc, 128)
        return np.ascontiguousarray(t.transpose(2, 0, 1, 3))

    Wq = qkv_w[0:2048]
    Wkv = qkv_w[2048:6144]
    wq = lhsT_tiles(Wq, 16, 16)
    wkvT = np.ascontiguousarray(Wkv.T)
    wkv = np.ascontiguousarray(
        wkvT.reshape(16, 128, 8, 512).transpose(2, 0, 1, 3))
    wp = lhsT_tiles(proj_w, 16, 16)
    wfc = lhsT_tiles(w_fc, 64, 16)
    wpf = lhsT_tiles(w_proj, 16, 64)

    t = np.arange(128, dtype=np.float32)
    diff = np.maximum(t[None, :] - t[:, None], 0.0).astype(np.float32)
    caus = (DSCALE * (t[None, :] >= t[:, None])).astype(np.float32)
    rowt = np.broadcast_to(t[None, :], (128, 128)).astype(np.float32)
    colc = np.stack([128.0 - t, np.full(128, 128.0, np.float32)],
                    axis=1).astype(np.float32)

    counts = np.array([qkv_w.size, proj_w.size, w_fc.size, w_proj.size],
                      np.float32)
    invn = (1.0 / counts).reshape(1, 4).astype(np.float32)

    in_maps = []
    for core in range(NCORE):
        b, p = core // 4, core % 4
        sl = np.ascontiguousarray(x[b, p * nt:(p + 1) * nt]).astype(np.float32)
        x_fm = np.ascontiguousarray(sl.T).reshape(16, 128, nt)
        econ = np.zeros((1, 8), np.float32)
        emask = np.zeros((1, 8), np.float32)
        for r in range(NCORE):
            br, pr = r // 4, r % 4
            if br == b and pr < p:
                econ[0, r] = nt * (p - 1 - pr)
                emask[0, r] = 1.0
        in_maps.append({
            "x_tok": sl,
            "x_fm": x_fm,
            "wq_t": wq, "wkv_t": wkv, "wp_t": wp, "wfc_t": wfc, "wpf_t": wpf,
            "qn_w": qkv_norm_w.reshape(1, D).astype(np.float32),
            "n_w": np.ascontiguousarray(norm_w.reshape(16, 128).T),
            "pn_w": np.ascontiguousarray(proj_norm_w.reshape(16, 128).T),
            "s_vec": s.reshape(1, H).astype(np.float32),
            "diffc": diff, "causc": caus, "rowtc": rowt, "colc": colc,
            "econ": econ, "emask": emask, "invn": invn,
            "cvec": np.array([[1e-5, EPS32, 1e-6, -0.5, 1.0 / D, 0, 0, 0]],
                             np.float32),
            "onecf": np.ones((128, 1), np.float32),
            "onecb": np.ones((128, 1), bfd),
            "onerf": np.ones((1, 128), np.float32),
        })
    return in_maps


def kernel(L0_attn_x, L0_attn_qkv_w, L0_attn_qkv_norm_w, L0_attn_proj_w,
           L0_attn_proj_norm_w, L0_attn_norm_w, L0_attn_s, L0_ffn_w_fc,
           L0_ffn_w_proj):
    x = np.asarray(L0_attn_x, np.float32)
    B, N, _ = x.shape
    nt = N // 4
    in_maps = make_inputs(x, np.asarray(L0_attn_qkv_w, np.float32),
                          np.asarray(L0_attn_qkv_norm_w, np.float32),
                          np.asarray(L0_attn_proj_w, np.float32),
                          np.asarray(L0_attn_proj_norm_w, np.float32),
                          np.asarray(L0_attn_norm_w, np.float32),
                          np.asarray(L0_attn_s, np.float32),
                          np.asarray(L0_ffn_w_fc, np.float32),
                          np.asarray(L0_ffn_w_proj, np.float32), nt)
    nc_a = build(nt, "A")
    res_a = run_bass_kernel_spmd(nc_a, in_maps, core_ids=list(range(NCORE)))
    seg_all = np.ascontiguousarray(np.concatenate(
        [np.asarray(res_a.results[r]["cc_in"]) for r in range(NCORE)], axis=0))
    in_maps_b = []
    for c in range(NCORE):
        m = dict(in_maps[c])
        for k in ("xh1_s", "q_s", "k_s", "v_s", "sums_io"):
            m[k] = np.asarray(res_a.results[c][k])
        m["cc_out"] = seg_all
        in_maps_b.append(m)
    nc_b = build(nt, "B")
    res_b = run_bass_kernel_spmd(nc_b, in_maps_b, core_ids=list(range(NCORE)))
    out = np.empty((B, N, D), np.float32)
    for core in range(NCORE):
        b, p = core // 4, core % 4
        o = np.asarray(res_b.results[core]["out_fm"]).reshape(D, nt)
        out[b, p * nt:(p + 1) * nt] = o.T
    return out
